# version 02
# baseline (speedup 1.0000x reference)
import contextlib

import numpy as np

import concourse.bass as bass
import concourse.mybir as mybir
from concourse.bass_utils import run_bass_kernel_spmd

F32 = mybir.dt.float32
AF = mybir.ActivationFunctionType
OP = mybir.AluOpType
AX = mybir.AxisListType

B, T, C = 4, 2048, 2048
NG, GS, DH = 4, 4, 128
WIN, SINK = 512, 4
SCALE = float(DH) ** -0.5
NE = 32
DFF = 768
EPS = 1e-5
CCH = C // 128
NKV_TILES = 13
NKV = NKV_TILES * 128
NQT = 8
QT0 = 5
BIG = 1.0e38
NCORES = 8


def _rope_tables():
    half = DH // 2
    inv = 1.0 / (10000.0 ** (np.arange(half, dtype=np.float64) * 2.0 / DH))
    t = np.arange(T, dtype=np.float64)
    ang = inv[:, None] * t[None, :]
    cos = np.cos(ang)
    sin = np.sin(ang)
    cos_dup = np.concatenate([cos, cos], 0).astype(np.float32)
    sin_sgn = np.concatenate([-sin, sin], 0).astype(np.float32)
    return cos_dup, sin_sgn


def _glob_tokens(h):
    w0 = h * 1024 - 512
    g = np.empty(NKV, dtype=np.int64)
    g[0:128] = np.arange(128)
    g[128:] = w0 + np.arange(12 * 128)
    return w0, g


def _masks_for_core(h):
    w0, _ = _glob_tokens(h)
    out = np.empty((128, NQT * 5 * 128), dtype=np.float32)
    j = np.arange(128)[:, None]
    i = np.arange(128)[None, :]
    for qt in range(NQT):
        qg = h * 1024 + 128 * qt
        for s in range(5):
            kg = w0 + 128 * (qt + s)
            kj = kg + j
            qi = qg + i
            keep = (kj >= 0) & (qi >= kj) & (((qi - kj) <= WIN) | (kj < SINK))
            out[:, (qt * 5 + s) * 128:(qt * 5 + s + 1) * 128] = np.where(
                keep, BIG, -BIG).astype(np.float32)
    return out


def _sinkbias_for_core(h):
    w0, _ = _glob_tokens(h)
    out = np.zeros((4, NQT), dtype=np.float32)
    for qt in range(NQT):
        if any(w0 + 128 * (qt + s) == 0 for s in range(5)):
            out[:, qt] = -BIG
    return out


def _xloc_for_core(x_b, h):
    rows = [x_b[0:128]]
    if h == 0:
        rows.append(np.zeros((512, C), dtype=x_b.dtype))
        rows.append(x_b[0:1024])
    else:
        rows.append(x_b[512:2048])
    return np.ascontiguousarray(np.concatenate(rows, axis=0))


class Sem:
    def __init__(self, stack, nc, name):
        self.h = stack.enter_context(nc.semaphore(name))
        self.n = 0


class SB:

    def __init__(self, handle, cols):
        self.h = handle
        self.cols = cols
        self.freelist = [(0, cols)]
        self.live = {}
        self.dead = {}

    def alloc(self, name, n):
        assert name not in self.live and name not in self.dead
        for idx, (c0, sz) in enumerate(self.freelist):
            if sz >= n:
                self.live[name] = (c0, n)
                if sz == n:
                    self.freelist.pop(idx)
                else:
                    self.freelist[idx] = (c0 + n, sz - n)
                return
        raise RuntimeError(f"SBUF arena overflow: {name} needs {n} cols; "
                           f"free={self.freelist}")

    def release(self, *names):
        for name in names:
            c0, n = self.live.pop(name)
            self.dead[name] = (c0, n)
            self.freelist.append((c0, n))
        self.freelist.sort()
        merged = []
        for c, s in self.freelist:
            if merged and merged[-1][0] + merged[-1][1] == c:
                merged[-1] = (merged[-1][0], merged[-1][1] + s)
            else:
                merged.append((c, s))
        self.freelist = merged

    def ap(self, name, col=0, p0=0, np_=128, n=1):
        c0, tot = self.live.get(name) or self.dead[name]
        assert col + n <= tot, (name, col, n, tot)
        return self.h[p0:p0 + np_, c0 + col:c0 + col + n]


class Emit:

    ENGINES = ("sync", "tensor", "vector", "scalar", "gpsimd")

    def __init__(self):
        self.ops = {k: [] for k in self.ENGINES}
        self.lastwait = {k: {} for k in self.ENGINES}

    def op(self, engine, fn, waits=(), inc=None, inc_by=1):
        waits = [(s, v) for (s, v) in waits if v > 0]
        for s, v in waits:
            assert v <= s.n, f"wait on future count {v} > {s.n}"
        if inc is not None:
            inc.n += inc_by
        lw = self.lastwait[engine]
        eff = []
        for s, v in waits:
            if lw.get(id(s), 0) < v:
                lw[id(s)] = v
                eff.append((s, v))

        def run(eng, fn=fn, eff=tuple(eff), inc=inc, inc_by=inc_by):
            for s, v in eff:
                eng.wait_ge(s.h, v)
            inst = fn(eng)
            if inc is not None:
                inst.then_inc(inc.h, inc_by)

        self.ops[engine].append(run)

    def replay(self, engine, eng):
        needs_drain = engine in ("vector", "scalar")
        for i, fn in enumerate(self.ops[engine]):
            if needs_drain and i > 0:
                eng.drain()
            fn(eng)


class DmaQ:

    def __init__(self, stack, nc, name, engine, em, pool):
        self.sems = [Sem(stack, nc, f"{name}{i}") for i in range(pool)]
        self.engine = engine
        self.em = em
        self.i = 0

    def issue(self, out_ap, in_ap, waits=()):
        s = self.sems[self.i % len(self.sems)]
        w = list(waits)
        if s.n:
            w.append((s, s.n))
        self.em.op(self.engine,
                   lambda e, o=out_ap, i=in_ap: e.dma_start(out=o, in_=i),
                   waits=w, inc=s, inc_by=16)
        self.i += 1
        return (s, s.n)


def build_program():
    nc = bass.Bass("TRN2", target_bir_lowering=False)

    d_x = nc.dram_tensor("xloc", [NKV, C], F32, kind="ExternalInput")
    d_wq = nc.dram_tensor("wq", [C, C], F32, kind="ExternalInput")
    d_wk = nc.dram_tensor("wk", [C, 512], F32, kind="ExternalInput")
    d_wv = nc.dram_tensor("wv", [C, 512], F32, kind="ExternalInput")
    d_wo = nc.dram_tensor("wo", [C, C], F32, kind="ExternalInput")
    d_bq = nc.dram_tensor("bqc", [128, CCH], F32, kind="ExternalInput")
    d_bk = nc.dram_tensor("bkc", [128, NG], F32, kind="ExternalInput")
    d_bv = nc.dram_tensor("bvr", [1, 512], F32, kind="ExternalInput")
    d_bo = nc.dram_tensor("boc", [128, CCH], F32, kind="ExternalInput")
    d_gateT = nc.dram_tensor("gateT", [C, NE], F32, kind="ExternalInput")
    d_wguT = nc.dram_tensor("wguT", [NE, C, 2 * DFF], F32,
                            kind="ExternalInput")
    d_wdT = nc.dram_tensor("wdT", [NE, DFF, C], F32, kind="ExternalInput")
    d_simB = nc.dram_tensor("simB", [128, NE * NE], F32,
                            kind="ExternalInput")
    d_masks = nc.dram_tensor("masks", [128, NQT * 5 * 128], F32,
                             kind="ExternalInput")
    d_sinkb = nc.dram_tensor("sinkb", [4, NQT], F32, kind="ExternalInput")
    d_cos = nc.dram_tensor("cosdup", [128, NKV], F32, kind="ExternalInput")
    d_sin = nc.dram_tensor("sinsgn", [128, NKV], F32, kind="ExternalInput")
    d_id = nc.dram_tensor("ident", [128, 128], F32, kind="ExternalInput")
    d_rot = nc.dram_tensor("rotmat", [128, 128], F32, kind="ExternalInput")
    d_oc = nc.dram_tensor("onescol", [128, 1], F32, kind="ExternalInput")
    d_or = nc.dram_tensor("onesrow", [1, 128], F32, kind="ExternalInput")
    d_out = nc.dram_tensor("out", [1024, C], F32, kind="ExternalOutput")

    TOTAL = 51712
    stack = contextlib.ExitStack()
    sb_h = stack.enter_context(nc.sbuf_tensor([128, TOTAL], F32))
    ps_h = stack.enter_context(nc.psum_tensor([128, 4096], F32))
    sb = SB(sb_h, TOTAL)
    em = Emit()

    PE = Sem(stack, nc, "sPE")
    DV = Sem(stack, nc, "sDV")
    AC = Sem(stack, nc, "sAC")
    qld = DmaQ(stack, nc, "qL", "sync", em, pool=8)
    qst = DmaQ(stack, nc, "qS", "gpsimd", em, pool=4)

    def bank(b, off=0, n=512, p0=0, np_=128):
        assert 0 <= b < 8 and off + n <= 512
        return ps_h[p0:p0 + np_, b * 512 + off:b * 512 + off + n]

    def wide(b, n, p0=0, np_=128):
        return ps_h[p0:p0 + np_, b * 512:b * 512 + n]

    bankdep = [None] * 8

    def bw(*bs):
        return [bankdep[b] for b in bs if bankdep[b] is not None]

    def bset(b, sem):
        bankdep[b] = (sem, sem.n)

    def dma(out_ap, in_ap, waits=(), q=None):
        return (q or qld).issue(out_ap, in_ap, waits)

    for nm, sz in [("c_id", 128), ("c_rot", 128), ("c_oc", 1), ("c_or", 128),
                   ("c_gate", CCH * NE), ("c_simB", NE * NE), ("c_snk", NQT),
                   ("c_bq", CCH), ("c_bo", CCH), ("c_bk", NG), ("c_bv", 512),
                   ("c_cos", NKV), ("c_sin", NKV)]:
        sb.alloc(nm, sz)
    A = sb.ap
    ctoks = [
        dma(A("c_id", 0, 0, 128, 128), d_id[:, :]),
        dma(A("c_rot", 0, 0, 128, 128), d_rot[:, :]),
        dma(A("c_oc", 0, 0, 128, 1), d_oc[:, :]),
        dma(A("c_or", 0, 0, 1, 128), d_or[:, :]),
        dma(A("c_gate", 0, 0, 128, CCH * NE).rearrange(
            "p (cc ee) -> p cc ee", ee=NE),
            d_gateT[:, :].rearrange("(cc p) ee -> p cc ee", p=128)),
        dma(A("c_simB", 0, 0, 128, NE * NE), d_simB[:, :]),
        dma(A("c_snk", 0, 0, 4, NQT), d_sinkb[:, :]),
        dma(A("c_bq", 0, 0, 128, CCH), d_bq[:, :]),
        dma(A("c_bo", 0, 0, 128, CCH), d_bo[:, :]),
        dma(A("c_bk", 0, 0, 128, NG), d_bk[:, :]),
        dma(A("c_bv", 0, 0, 1, 512), d_bv[:, :]),
        dma(A("c_cos", 0, 0, 128, NKV), d_cos[:, :]),
        dma(A("c_sin", 0, 0, 128, NKV), d_sin[:, :]),
    ]

    sb.alloc("xTh", CCH * 5 * 128)
    sb.alloc("xTo", CCH * 1024)
    sb.alloc("K", NG * NKV)
    sb.alloc("V", NKV_TILES * 512)

    def xT(cc, kvcol, n, p0=0, np_=128):
        if kvcol < 5 * 128:
            assert kvcol + n <= 5 * 128
            return A("xTh", cc * 640 + kvcol, p0, np_, n)
        assert kvcol >= 5 * 128 and kvcol + n <= NKV
        return A("xTo", cc * 1024 + (kvcol - 640), p0, np_, n)

    def Kap(g, kvcol, n, p0=0, np_=128):
        return A("K", g * NKV + kvcol, p0, np_, n)

    def Vap(tile, d0, n, p0=0, np_=128):
        return A("V", tile * 512 + d0, p0, np_, n)

    sb.alloc("xtile", C)
    sb.alloc("wkv", 2048)
    sb.alloc("kraw", 128)
    sb.alloc("rs1", 128)
    sb.alloc("rs2", 128)

    xT_done = []
    xtile_use = 0
    for t in range(NKV_TILES):
        tok_x = dma(A("xtile", 0, 0, 128, C), d_x[t * 128:(t + 1) * 128, :],
                    waits=[(PE, xtile_use)])
        for cc in range(CCH):
            b = cc % 2
            em.op("tensor",
                  lambda e, cc=cc, b=b: e.transpose(
                      bank(b, 0, 128),
                      A("xtile", cc * 128, 0, 128, 128),
                      A("c_id", 0, 0, 128, 128)),
                  waits=[tok_x] + ctoks + bw(b), inc=PE)
            em.op("scalar",
                  lambda e, cc=cc, t=t, b=b: e.copy(
                      xT(cc, t * 128, 128), bank(b, 0, 128)),
                  waits=[(PE, PE.n)], inc=AC)
            bset(b, AC)
        xtile_use = PE.n
        xT_done.append(AC.n)

    wkv_use = 0
    for g in range(NG):
        tok_wk = dma(A("wkv", 0, 0, 128, 2048).rearrange(
            "p (cc d) -> p cc d", d=128),
            d_wk[:, g * 128:(g + 1) * 128].rearrange(
                "(cc p) d -> p cc d", p=128),
            waits=[(PE, wkv_use)])
        for t in range(NKV_TILES):
            kb = 2 + t % 2
            for cc in range(CCH):
                w = ([tok_wk, (AC, xT_done[t])] + bw(kb)) if cc == 0 \
                    else ()
                em.op("tensor",
                      lambda e, cc=cc, t=t, kb=kb: e.matmul(
                          bank(kb, 0, 128),
                          A("wkv", cc * 128, 0, 128, 128),
                          xT(cc, t * 128, 128),
                          start=(cc == 0), stop=(cc == CCH - 1)),
                      waits=w, inc=(PE if cc == CCH - 1 else None))
            mmk = PE.n
            em.op("scalar",
                  lambda e, g=g, kb=kb: e.activation(
                      A("kraw", 0, 0, 128, 128), bank(kb, 0, 128),
                      AF.Identity, bias=A("c_bk", g, 0, 128, 1)),
                  waits=[(PE, mmk), (DV, DV.n)] + ctoks, inc=AC)
            bset(kb, AC)
            rb = 4 + t % 2
            em.op("tensor",
                  lambda e, rb=rb: e.matmul(
                      bank(rb, 0, 128), A("c_rot", 0, 0, 128, 128),
                      A("kraw", 0, 0, 128, 128), start=True, stop=True),
                  waits=[(AC, AC.n)] + bw(rb), inc=PE)
            rotk = PE.n
            em.op("vector",
                  lambda e, t=t: e.tensor_mul(
                      A("rs1", 0, 0, 128, 128), A("kraw", 0, 0, 128, 128),
                      A("c_cos", t * 128, 0, 128, 128)),
                  waits=[(AC, AC.n)] + ctoks)
            em.op("vector",
                  lambda e, t=t, rb=rb: e.tensor_mul(
                      A("rs2", 0, 0, 128, 128), bank(rb, 0, 128),
                      A("c_sin", t * 128, 0, 128, 128)),
                  waits=[(PE, rotk)])
            em.op("vector",
                  lambda e, g=g, t=t: e.tensor_add(
                      Kap(g, t * 128, 128), A("rs1", 0, 0, 128, 128),
                      A("rs2", 0, 0, 128, 128)),
                  inc=DV)
            bset(rb, DV)
        wkv_use = PE.n

    for vh in range(4):
        tok_wv = dma(A("wkv", 0, 0, 128, CCH * 128).rearrange(
            "p (cc d) -> p cc d", d=128),
            d_wv[:, vh * 128:(vh + 1) * 128].rearrange(
                "(cc p) d -> p cc d", p=128),
            waits=[(PE, wkv_use)])
        for t in range(NKV_TILES):
            vb = 6 + t % 2
            for cc in range(CCH):
                w = ([tok_wv, (AC, xT_done[t])] + bw(vb)) if cc == 0 \
                    else ()
                em.op("tensor",
                      lambda e, cc=cc, t=t, vb=vb: e.matmul(
                          bank(vb, 0, 128),
                          xT(cc, t * 128, 128),
                          A("wkv", cc * 128, 0, 128, 128),
                          start=(cc == 0), stop=False),
                      waits=w)
            em.op("tensor",
                  lambda e, vh=vh, vb=vb: e.matmul(
                      bank(vb, 0, 128), A("c_or", 0, 0, 1, 128),
                      A("c_bv", vh * 128, 0, 1, 128),
                      start=False, stop=True),
                  inc=PE)
            em.op("scalar",
                  lambda e, t=t, vh=vh, vb=vb: e.copy(
                      Vap(t, vh * 128, 128), bank(vb, 0, 128)),
                  waits=[(PE, PE.n)], inc=AC)
            bset(vb, AC)
        wkv_use = PE.n

    sb.release("xtile", "wkv", "kraw", "rs1", "rs2", "xTh")
    fenceB = [(PE, PE.n), (DV, DV.n), (AC, AC.n)]

    sb.alloc("masks", NQT * 5 * 128)
    sb.alloc("Q", CCH * 128)
    sb.alloc("attnT", CCH * 128)
    sb.alloc("PT", 5 * 128)
    sb.alloc("PTs", 128)
    sb.alloc("mskscr", 5 * 128)
    sb.alloc("wb", 2 * 2048)
    sb.alloc("qraw", 128)
    sb.alloc("qs1", 128)
    sb.alloc("qs2", 128)
    sb.alloc("sqs", 128)
    sb.alloc("rows", 16)
    sb.alloc("recrow", 128)
    sb.alloc("recbc", 128)
    sb.alloc("lnbc", 256)

    tok_masks = dma(A("masks", 0, 0, 128, NQT * 5 * 128), d_masks[:, :],
                    waits=fenceB)

    wb_use = [0, 0]
    for qt in range(NQT):
        qcol = (QT0 + qt) * 128

        for dc in range(CCH):
            par = dc % 2
            tok_wb = dma(A("wb", par * 2048, 0, 128, 2048).rearrange(
                "p (cc d) -> p cc d", d=128),
                d_wq[:, dc * 128:(dc + 1) * 128].rearrange(
                    "(cc p) d -> p cc d", p=128),
                waits=[(PE, wb_use[par])] + fenceB)
            qb = dc % 2
            for cc in range(CCH):
                w = ([tok_wb] + bw(qb)) if cc == 0 else ()
                em.op("tensor",
                      lambda e, cc=cc, par=par, qb=qb, qcol=qcol: e.matmul(
                          bank(qb, 0, 128),
                          A("wb", par * 2048 + cc * 128, 0, 128, 128),
                          xT(cc, qcol, 128),
                          start=(cc == 0), stop=(cc == CCH - 1)),
                      waits=w, inc=(PE if cc == CCH - 1 else None))
            wb_use[par] = PE.n
            mmq = PE.n
            em.op("scalar",
                  lambda e, dc=dc, qb=qb: e.activation(
                      A("qraw", 0, 0, 128, 128), bank(qb, 0, 128),
                      AF.Identity, bias=A("c_bq", dc, 0, 128, 1)),
                  waits=[(PE, mmq), (DV, DV.n)], inc=AC)
            bset(qb, AC)
            rb = 2 + dc % 2
            em.op("tensor",
                  lambda e, rb=rb: e.matmul(
                      bank(rb, 0, 128), A("c_rot", 0, 0, 128, 128),
                      A("qraw", 0, 0, 128, 128), start=True, stop=True),
                  waits=[(AC, AC.n)] + bw(rb), inc=PE)
            rotq = PE.n
            em.op("vector",
                  lambda e, qcol=qcol: e.tensor_mul(
                      A("qs1", 0, 0, 128, 128), A("qraw", 0, 0, 128, 128),
                      A("c_cos", qcol, 0, 128, 128)),
                  waits=[(AC, AC.n)])
            em.op("vector",
                  lambda e, rb=rb, qcol=qcol: e.tensor_mul(
                      A("qs2", 0, 0, 128, 128), bank(rb, 0, 128),
                      A("c_sin", qcol, 0, 128, 128)),
                  waits=[(PE, rotq)])
            em.op("vector",
                  lambda e, dc=dc: e.tensor_add(
                      A("Q", dc * 128, 0, 128, 128),
                      A("qs1", 0, 0, 128, 128), A("qs2", 0, 0, 128, 128)),
                  inc=DV)
            bset(rb, DV)
        q_ready = DV.n

        for g in range(NG):
            for hh in range(GS):
                gh = g * GS + hh
                pp = gh % 2
                bA, bB, bav, bdn = 2 * pp, 2 * pp + 1, 4 + pp, 6 + pp
                for s in range(5):
                    tb, toff = (bA, s * 128) if s < 4 else (bB, 0)
                    w = [(DV, q_ready), tok_masks] + bw(bA, bB) \
                        if s == 0 else ()
                    em.op("tensor",
                          lambda e, s=s, g=g, gh=gh, qt=qt, tb=tb,
                          toff=toff: e.matmul(
                              bank(tb, toff, 128),
                              Kap(g, (1 + qt + s) * 128, 128),
                              A("Q", gh * 128, 0, 128, 128),
                              start=True, stop=True),
                          waits=w)
                em.op("tensor",
                      lambda e, g=g, gh=gh, bB=bB: e.matmul(
                          bank(bB, 128, 128, 0, 4),
                          Kap(g, 0, 4),
                          A("Q", gh * 128, 0, 128, 128),
                          start=True, stop=True),
                      inc=PE)
                smm = PE.n
                em.op("vector",
                      lambda e, bA=bA, qt=qt: e.tensor_tensor(
                          A("mskscr", 0, 0, 128, 640),
                          wide(bA, 640),
                          A("masks", qt * 640, 0, 128, 640), OP.min),
                      waits=[(PE, smm), (AC, AC.n)], inc=DV)
                em.op("scalar",
                      lambda e: e.activation(
                          A("PT", 0, 0, 128, 640),
                          A("mskscr", 0, 0, 128, 640), AF.Exp, scale=SCALE),
                      waits=[(DV, DV.n)], inc=AC)
                em.op("scalar",
                      lambda e, qt=qt, bB=bB: e.activation(
                          A("PTs", 0, 0, 4, 128), bank(bB, 128, 128, 0, 4),
                          AF.Exp, bias=A("c_snk", qt, 0, 4, 1), scale=SCALE),
                      waits=[(PE, smm)], inc=AC)
                exp_done = AC.n
                bset(bA, AC)
                bset(bB, AC)
                for s in range(5):
                    w = ([(AC, exp_done)] + bw(bav, bdn)) if s == 0 else ()
                    em.op("tensor",
                          lambda e, s=s, g=g, qt=qt, bav=bav: e.matmul(
                              bank(bav, 0, 128),
                              Vap(1 + qt + s, g * 128, 128),
                              A("PT", s * 128, 0, 128, 128),
                              start=(s == 0), stop=False),
                          waits=w)
                em.op("tensor",
                      lambda e, g=g, bav=bav: e.matmul(
                          bank(bav, 0, 128),
                          Vap(0, g * 128, 128, 0, 4),
                          A("PTs", 0, 0, 4, 128),
                          start=False, stop=True))
                for s in range(5):
                    em.op("tensor",
                          lambda e, s=s, bdn=bdn: e.matmul(
                              bank(bdn, 0, 128, 0, 1),
                              A("c_oc", 0, 0, 128, 1),
                              A("PT", s * 128, 0, 128, 128),
                              start=(s == 0), stop=False))
                em.op("tensor",
                      lambda e, bdn=bdn: e.matmul(
                          bank(bdn, 0, 128, 0, 1),
                          A("c_oc", 0, 0, 4, 1),
                          A("PTs", 0, 0, 4, 128),
                          start=False, stop=True),
                      inc=PE)
                av_done = PE.n
                em.op("vector",
                      lambda e, bdn=bdn: e.reciprocal(
                          A("recrow", 0, 0, 1, 128), bank(bdn, 0, 128, 0, 1)),
                      waits=[(PE, av_done)], inc=DV)
                em.op("tensor",
                      lambda e, bdn=bdn: e.matmul(
                          bank(bdn, 128, 128), A("c_or", 0, 0, 1, 128),
                          A("recrow", 0, 0, 1, 128), start=True, stop=True),
                      waits=[(DV, DV.n)], inc=PE)
                em.op("vector",
                      lambda e, bdn=bdn: e.tensor_copy(
                          A("recbc", 0, 0, 128, 128), bank(bdn, 128, 128)),
                      waits=[(PE, PE.n)])
                em.op("vector",
                      lambda e, gh=gh, bav=bav: e.tensor_mul(
                          A("attnT", gh * 128, 0, 128, 128),
                          bank(bav, 0, 128), A("recbc", 0, 0, 128, 128)),
                      inc=DV)
                bset(bav, DV)
                bset(bdn, DV)
        attn_done = DV.n

        for mt in range(CCH):
            par = mt % 2
            tok_wb = dma(A("wb", par * 2048, 0, 128, 2048).rearrange(
                "p (cc d) -> p cc d", d=128),
                d_wo[:, mt * 128:(mt + 1) * 128].rearrange(
                    "(cc p) d -> p cc d", p=128),
                waits=[(PE, wb_use[par])] + fenceB)
            ob = mt % 2
            for cc in range(CCH):
                w = ([tok_wb, (DV, attn_done)] + bw(ob)) if cc == 0 \
                    else ()
                em.op("tensor",
                      lambda e, cc=cc, par=par, ob=ob: e.matmul(
                          bank(ob, 0, 128),
                          A("wb", par * 2048 + cc * 128, 0, 128, 128),
                          A("attnT", cc * 128, 0, 128, 128),
                          start=(cc == 0), stop=(cc == CCH - 1)),
                      waits=w, inc=(PE if cc == CCH - 1 else None))
            wb_use[par] = PE.n
            em.op("vector",
                  lambda e, mt=mt, ob=ob, qcol=qcol: e.scalar_tensor_tensor(
                      xT(mt, qcol, 128), bank(ob, 0, 128),
                      A("c_bo", mt, 0, 128, 1), xT(mt, qcol, 128),
                      OP.add, OP.add),
                  waits=[(PE, PE.n)], inc=DV)
            bset(ob, DV)
        r1_done = DV.n

        for cc in range(CCH):
            w = ([(DV, r1_done)] + bw(4)) if cc == 0 else ()
            em.op("tensor",
                  lambda e, cc=cc, qcol=qcol: e.matmul(
                      bank(4, 0, 128, 0, 1), A("c_oc", 0, 0, 128, 1),
                      xT(cc, qcol, 128),
                      start=(cc == 0), stop=(cc == CCH - 1)),
                  waits=w, inc=(PE if cc == CCH - 1 else None))
        for cc in range(CCH):
            em.op("scalar",
                  lambda e, cc=cc, qcol=qcol: e.activation(
                      A("sqs", 0, 0, 128, 128), xT(cc, qcol, 128),
                      AF.Square),
                  waits=([(DV, r1_done), (PE, PE.n)] if cc == 0
                         else [(PE, PE.n)]),
                  inc=AC)
            w = ([(AC, AC.n)] + bw(5)) if cc == 0 else [(AC, AC.n)]
            em.op("tensor",
                  lambda e, cc=cc: e.matmul(
                      bank(5, 0, 128, 0, 1), A("c_oc", 0, 0, 128, 1),
                      A("sqs", 0, 0, 128, 128),
                      start=(cc == 0), stop=(cc == CCH - 1)),
                  waits=w, inc=PE)
        stat_mm = PE.n
        em.op("vector",
              lambda e: e.tensor_scalar(
                  A("qs1", 0, 0, 1, 128), bank(4, 0, 128, 0, 1),
                  1.0 / C, None, OP.mult),
              waits=[(PE, stat_mm)])
        em.op("vector",
              lambda e: e.tensor_scalar(
                  A("qs2", 0, 0, 1, 128), bank(5, 0, 128, 0, 1),
                  1.0 / C, None, OP.mult))
        em.op("vector",
              lambda e: e.tensor_mul(
                  A("sqs", 0, 0, 1, 128), A("qs1", 0, 0, 1, 128),
                  A("qs1", 0, 0, 1, 128)))
        em.op("vector",
              lambda e: e.tensor_sub(
                  A("qs2", 0, 0, 1, 128), A("qs2", 0, 0, 1, 128),
                  A("sqs", 0, 0, 1, 128)))
        em.op("vector",
              lambda e: e.tensor_scalar(
                  A("qs2", 0, 0, 1, 128), A("qs2", 0, 0, 1, 128),
                  EPS, None, OP.add))
        em.op("vector",
              lambda e: e.reciprocal(
                  A("sqs", 0, 0, 1, 128), A("qs2", 0, 0, 1, 128)),
              inc=DV)
        em.op("scalar",
              lambda e: e.activation(
                  A("qs2", 0, 0, 1, 128), A("sqs", 0, 0, 1, 128), AF.Sqrt),
              waits=[(DV, DV.n)], inc=AC)
        em.op("vector",
              lambda e: e.tensor_mul(
                  A("sqs", 0, 0, 1, 128), A("qs1", 0, 0, 1, 128),
                  A("qs2", 0, 0, 1, 128)),
              waits=[(AC, AC.n)])
        em.op("vector",
              lambda e: e.tensor_scalar(
                  A("sqs", 0, 0, 1, 128), A("sqs", 0, 0, 1, 128),
                  -1.0, None, OP.mult),
              inc=DV)
        ln_rows = DV.n
        em.op("tensor",
              lambda e: e.matmul(
                  bank(4, 0, 128), A("c_or", 0, 0, 1, 128),
                  A("qs2", 0, 0, 1, 128), start=True, stop=True),
              waits=[(DV, ln_rows)] + bw(4))
        em.op("tensor",
              lambda e: e.matmul(
                  bank(5, 0, 128), A("c_or", 0, 0, 1, 128),
                  A("sqs", 0, 0, 1, 128), start=True, stop=True),
              waits=bw(5), inc=PE)
        em.op("vector",
              lambda e: e.tensor_copy(A("lnbc", 0, 0, 128, 128),
                                      bank(4, 0, 128)),
              waits=[(PE, PE.n)])
        em.op("vector",
              lambda e: e.tensor_copy(A("lnbc", 128, 0, 128, 128),
                                      bank(5, 0, 128)),
              inc=DV)
        bset(4, DV)
        bset(5, DV)
        for cc in range(CCH):
            em.op("vector",
                  lambda e, cc=cc, qcol=qcol: e.tensor_mul(
                      A("qs1", 0, 0, 128, 128),
                      xT(cc, qcol, 128), A("lnbc", 0, 0, 128, 128)),
                  waits=([(DV, DV.n)] if cc == 0 else ()))
            em.op("vector",
                  lambda e, cc=cc, qcol=qcol: e.tensor_add(
                      xT(cc, qcol, 128), A("qs1", 0, 0, 128, 128),
                      A("lnbc", 128, 0, 128, 128)),
                  inc=DV)

    sb.release("masks", "Q", "attnT", "PT", "PTs", "mskscr", "wb", "qraw",
               "qs1", "qs2", "sqs", "rows", "recrow", "recbc", "lnbc",
               "c_cos", "c_sin", "K", "V")
    z_done = DV.n
    fenceC = [(PE, PE.n), (DV, DV.n), (AC, AC.n)]

    sb.alloc("probs", NQT * NE)
    sb.alloc("prim", NQT * NE)
    sb.alloc("sec", NQT * NE)
    sb.alloc("rw", NQT * NE)
    sb.alloc("bs", NQT * NE)
    sb.alloc("comb", NQT * NE)
    sb.alloc("mx", NQT * 8)
    sb.alloc("rcol", 4)
    sb.alloc("rscr", 2 * NE)
    sb.alloc("cwrow", 512)

    for r in range(NQT):
        qcol = (QT0 + r) * 128
        lb = r % 2
        for cc in range(CCH):
            w = ([(DV, z_done)] + bw(lb)) if cc == 0 else ()
            em.op("tensor",
                  lambda e, cc=cc, lb=lb, qcol=qcol: e.matmul(
                      bank(lb, 0, NE),
                      xT(cc, qcol, 128),
                      A("c_gate", cc * NE, 0, 128, NE),
                      start=(cc == 0), stop=(cc == CCH - 1)),
                  waits=w, inc=(PE if cc == CCH - 1 else None))
        em.op("vector",
              lambda e, lb=lb: e.reduce_max(
                  A("rcol", 0, 0, 128, 1), bank(lb, 0, NE), AX.X,
                  negate=True),
              waits=[(PE, PE.n)], inc=DV)
        em.op("scalar",
              lambda e, r=r, lb=lb: e.activation(
                  A("probs", r * NE, 0, 128, NE), bank(lb, 0, NE), AF.Exp,
                  bias=A("rcol", 0, 0, 128, 1),
                  accum_out=A("rcol", 1, 0, 128, 1)),
              waits=[(DV, DV.n)], inc=AC)
        bset(lb, AC)
        em.op("vector",
              lambda e: e.reciprocal(A("rcol", 2, 0, 128, 1),
                                     A("rcol", 1, 0, 128, 1)),
              waits=[(AC, AC.n)])
        em.op("vector",
              lambda e, r=r: e.tensor_scalar(
                  A("probs", r * NE, 0, 128, NE), A("probs", r * NE, 0, 128, NE),
                  A("rcol", 2, 0, 128, 1), None, OP.mult))
        em.op("vector",
              lambda e, r=r: e.max(A("mx", r * 8, 0, 128, 8),
                                   A("probs", r * NE, 0, 128, NE)))
        em.op("vector",
              lambda e, r=r: e.reduce_sum(
                  A("rcol", 0, 0, 128, 1), A("mx", r * 8, 0, 128, 8), AX.X))
        em.op("vector",
              lambda e: e.reciprocal(A("rcol", 1, 0, 128, 1),
                                     A("rcol", 0, 0, 128, 1)))
        em.op("vector",
              lambda e, r=r: e.tensor_scalar(
                  A("prim", r * NE, 0, 128, NE),
                  A("probs", r * NE, 0, 128, NE),
                  A("mx", r * 8 + 3, 0, 128, 1), None, OP.is_ge))
        em.op("vector",
              lambda e, r=r: e.tensor_scalar(
                  A("rscr", 0, 0, 128, NE), A("probs", r * NE, 0, 128, NE),
                  A("mx", r * 8 + 7, 0, 128, 1), None, OP.is_ge))
        em.op("vector",
              lambda e, r=r: e.tensor_sub(
                  A("sec", r * NE, 0, 128, NE), A("rscr", 0, 0, 128, NE),
                  A("prim", r * NE, 0, 128, NE)))
        em.op("vector",
              lambda e, r=r: e.tensor_scalar(
                  A("rw", r * NE, 0, 128, NE), A("probs", r * NE, 0, 128, NE),
                  A("rcol", 1, 0, 128, 1), None, OP.mult))
        em.op("vector",
              lambda e, r=r: e.tensor_mul(
                  A("rw", r * NE, 0, 128, NE), A("rw", r * NE, 0, 128, NE),
                  A("rscr", 0, 0, 128, NE)))
        for s in range(NE):
            em.op("vector",
                  lambda e, r=r, s=s: e.tensor_mul(
                      A("rscr", NE, 0, 128, NE), A("prim", r * NE, 0, 128, NE),
                      A("c_simB", s * NE, 0, 128, NE)))
            em.op("vector",
                  lambda e, r=r, s=s: e.reduce_max(
                      A("bs", r * NE + s, 0, 128, 1),
                      A("rscr", NE, 0, 128, NE), AX.X))
        em.op("vector",
              lambda e, r=r: e.tensor_scalar(
                  A("rscr", 0, 0, 128, NE), A("bs", r * NE, 0, 128, NE),
                  0.5, None, OP.is_lt))
        em.op("vector",
              lambda e, r=r: e.tensor_mul(
                  A("rscr", NE, 0, 128, NE), A("sec", r * NE, 0, 128, NE),
                  A("rscr", 0, 0, 128, NE)))
        em.op("vector",
              lambda e, r=r: e.tensor_add(
                  A("rscr", NE, 0, 128, NE), A("rscr", NE, 0, 128, NE),
                  A("prim", r * NE, 0, 128, NE)))
        em.op("vector",
              lambda e, r=r: e.tensor_mul(
                  A("comb", r * NE, 0, 128, NE), A("rw", r * NE, 0, 128, NE),
                  A("rscr", NE, 0, 128, NE)))
        em.op("vector",
              lambda e, r=r: e.tensor_scalar(
                  A("rscr", 0, 0, 128, NE), A("rscr", 0, 0, 128, NE),
                  -1.0, 1.0, OP.mult, OP.add))
        em.op("vector",
              lambda e, r=r: e.tensor_mul(
                  A("sec", r * NE, 0, 128, NE), A("sec", r * NE, 0, 128, NE),
                  A("rw", r * NE, 0, 128, NE)))
        em.op("vector",
              lambda e, r=r: e.tensor_mul(
                  A("sec", r * NE, 0, 128, NE), A("sec", r * NE, 0, 128, NE),
                  A("rscr", 0, 0, 128, NE)))
        for s in range(NE):
            em.op("vector",
                  lambda e, r=r, s=s: e.tensor_scalar(
                      A("rscr", 0, 0, 128, NE), A("c_simB", s * NE, 0, 128, NE),
                      A("bs", r * NE + s, 0, 128, 1), None, OP.is_ge))
            em.op("vector",
                  lambda e, r=r: e.tensor_mul(
                      A("rscr", 0, 0, 128, NE), A("rscr", 0, 0, 128, NE),
                      A("prim", r * NE, 0, 128, NE)))
            em.op("vector",
                  lambda e, r=r, s=s: e.scalar_tensor_tensor(
                      A("comb", r * NE, 0, 128, NE),
                      A("rscr", 0, 0, 128, NE),
                      A("sec", r * NE + s, 0, 128, 1),
                      A("comb", r * NE, 0, 128, NE),
                      OP.mult, OP.add),
                  inc=(DV if s == NE - 1 else None))
    routing_done = DV.n

    sb.release("probs", "prim", "rw", "bs", "mx", "rcol", "rscr", "sec")

    sb.alloc("wgu", 2 * CCH * 128)
    sb.alloc("wd", 2 * 6 * 256)
    sb.alloc("sg", 6 * 512)
    sb.alloc("hh", 6 * 512)
    sb.alloc("y", CCH * 512)
    sb.alloc("cw", 512)
    sb.alloc("sq2", 512)
    sb.alloc("lrow", 3 * 512)
    sb.alloc("lbc", 1024)
    sb.alloc("fin", 512)
    sb.alloc("otm", C)

    wgu_use = [0, 0]
    wd_use = [0, 0]
    tok_st = None
    for ih in range(2):
        icol0 = QT0 * 128 + ih * 512

        def zap(cc, n=512, ih=ih):
            return xT(cc, QT0 * 128 + ih * 512 - 0, n) if False else \
                A("xTo", cc * 1024 + ih * 512, 0, 128, n)

        for e_ix in range(NE):
            for rr in range(4):
                em.op("tensor",
                      lambda e, rr=rr, e_ix=e_ix, ih=ih: e.transpose(
                          bank(4, rr * 128, 128, 0, 1),
                          A("comb", (ih * 4 + rr) * NE + e_ix, 0, 128, 1),
                          A("c_id", 0, 0, 128, 128)),
                      waits=([(DV, routing_done)] + bw(4)) if rr == 0
                      else (), inc=(PE if rr == 3 else None))
            em.op("scalar",
                  lambda e: e.copy(A("cwrow", 0, 0, 1, 512),
                                   bank(4, 0, 512, 0, 1)),
                  waits=[(PE, PE.n)], inc=AC)
            em.op("tensor",
                  lambda e: e.matmul(
                      bank(4, 0, 512), A("c_or", 0, 0, 1, 128),
                      A("cwrow", 0, 0, 1, 512),
                      start=True, stop=True),
                  waits=[(AC, AC.n)], inc=PE)
            em.op("vector",
                  lambda e: e.tensor_copy(A("cw", 0, 0, 128, 512),
                                          bank(4, 0, 512)),
                  waits=[(PE, PE.n)], inc=DV)
            bset(4, DV)
            cw_done = DV.n
            for ft in range(12):
                par = ft % 2
                tok_g = dma(
                    A("wgu", par * CCH * 128, 0, 128, CCH * 128).rearrange(
                        "p (cc f) -> p cc f", f=128),
                    d_wguT[e_ix, :, ft * 128:(ft + 1) * 128].rearrange(
                        "(cc p) f -> p cc f", p=128),
                    waits=[(PE, wgu_use[par])] + fenceC)
                gb = ft % 2
                for cc in range(CCH):
                    w = ([tok_g] + bw(gb)) if cc == 0 else ()
                    em.op("tensor",
                          lambda e, cc=cc, par=par, gb=gb,
                          zap=zap: e.matmul(
                              bank(gb, 0, 512),
                              A("wgu", par * CCH * 128 + cc * 128,
                                0, 128, 128),
                              zap(cc),
                              start=(cc == 0), stop=(cc == CCH - 1)),
                          waits=w, inc=(PE if cc == CCH - 1 else None))
                wgu_use[par] = PE.n
                gmm = PE.n
                if ft < 6:
                    em.op("scalar",
                          lambda e, ft=ft, gb=gb: e.activation(
                              A("sg", ft * 512, 0, 128, 512),
                              bank(gb, 0, 512), AF.Silu),
                          waits=[(PE, gmm), (DV, DV.n)], inc=AC)
                    bset(gb, AC)
                else:
                    k = ft - 6
                    em.op("vector",
                          lambda e, k=k, gb=gb: e.tensor_mul(
                              A("hh", k * 512, 0, 128, 512),
                              A("sg", k * 512, 0, 128, 512),
                              bank(gb, 0, 512)),
                          waits=[(PE, gmm), (AC, AC.n), (DV, cw_done)])
                    em.op("vector",
                          lambda e, k=k: e.tensor_mul(
                              A("hh", k * 512, 0, 128, 512),
                              A("hh", k * 512, 0, 128, 512),
                              A("cw", 0, 0, 128, 512)),
                          inc=DV)
                    bset(gb, DV)
            h_done = DV.n
            for mp in range(8):
                par = mp % 2
                tok_d = dma(
                    A("wd", par * 6 * 256, 0, 128, 6 * 256).rearrange(
                        "p (fc m) -> p fc m", m=256),
                    d_wdT[e_ix, :, mp * 256:(mp + 1) * 256].rearrange(
                        "(fc p) m -> p fc m", p=128),
                    waits=[(PE, wd_use[par])] + fenceC)
                for sub in range(2):
                    mt = mp * 2 + sub
                    db = 2 + mt % 2
                    for fc in range(6):
                        w = ([tok_d, (DV, h_done)] + bw(db)) \
                            if fc == 0 else ()
                        em.op("tensor",
                              lambda e, fc=fc, par=par, sub=sub,
                              db=db: e.matmul(
                                  bank(db, 0, 512),
                                  A("wd", par * 6 * 256 + fc * 256
                                    + sub * 128, 0, 128, 128),
                                  A("hh", fc * 512, 0, 128, 512),
                                  start=(fc == 0), stop=(fc == 5)),
                              waits=w, inc=(PE if fc == 5 else None))
                    wd_use[par] = PE.n
                    if e_ix == 0:
                        em.op("vector",
                              lambda e, mt=mt, db=db: e.tensor_copy(
                                  A("y", mt * 512, 0, 128, 512),
                                  bank(db, 0, 512)),
                              waits=[(PE, PE.n)], inc=DV)
                    else:
                        em.op("vector",
                              lambda e, mt=mt, db=db: e.tensor_add(
                                  A("y", mt * 512, 0, 128, 512),
                                  A("y", mt * 512, 0, 128, 512),
                                  bank(db, 0, 512)),
                              waits=[(PE, PE.n)], inc=DV)
                    bset(db, DV)

        for mt in range(CCH):
            em.op("vector",
                  lambda e, mt=mt, zap=zap: e.tensor_add(
                      A("y", mt * 512, 0, 128, 512),
                      A("y", mt * 512, 0, 128, 512), zap(mt)),
                  inc=DV)
        r2_done = DV.n
        for cc in range(CCH):
            w = ([(DV, r2_done)] + bw(5)) if cc == 0 else ()
            em.op("tensor",
                  lambda e, cc=cc: e.matmul(
                      bank(5, 0, 512, 0, 1), A("c_oc", 0, 0, 128, 1),
                      A("y", cc * 512, 0, 128, 512),
                      start=(cc == 0), stop=(cc == CCH - 1)),
                  waits=w, inc=(PE if cc == CCH - 1 else None))
        for cc in range(CCH):
            em.op("scalar",
                  lambda e, cc=cc: e.activation(
                      A("sq2", 0, 0, 128, 512), A("y", cc * 512, 0, 128, 512),
                      AF.Square),
                  waits=([(DV, r2_done), (PE, PE.n)] if cc == 0
                         else [(PE, PE.n)]),
                  inc=AC)
            w = ([(AC, AC.n)] + bw(6)) if cc == 0 else [(AC, AC.n)]
            em.op("tensor",
                  lambda e, cc=cc: e.matmul(
                      bank(6, 0, 512, 0, 1), A("c_oc", 0, 0, 128, 1),
                      A("sq2", 0, 0, 128, 512),
                      start=(cc == 0), stop=(cc == CCH - 1)),
                  waits=w, inc=PE)
        st2 = PE.n
        em.op("vector",
              lambda e: e.tensor_scalar(
                  A("lrow", 0, 0, 1, 512), bank(5, 0, 512, 0, 1),
                  1.0 / C, None, OP.mult),
              waits=[(PE, st2)])
        em.op("vector",
              lambda e: e.tensor_scalar(
                  A("lrow", 512, 0, 1, 512), bank(6, 0, 512, 0, 1),
                  1.0 / C, None, OP.mult))
        em.op("vector",
              lambda e: e.tensor_mul(
                  A("lrow", 1024, 0, 1, 512), A("lrow", 0, 0, 1, 512),
                  A("lrow", 0, 0, 1, 512)))
        em.op("vector",
              lambda e: e.tensor_sub(
                  A("lrow", 512, 0, 1, 512), A("lrow", 512, 0, 1, 512),
                  A("lrow", 1024, 0, 1, 512)))
        em.op("vector",
              lambda e: e.tensor_scalar(
                  A("lrow", 512, 0, 1, 512), A("lrow", 512, 0, 1, 512),
                  EPS, None, OP.add))
        em.op("vector",
              lambda e: e.reciprocal(
                  A("lrow", 1024, 0, 1, 512), A("lrow", 512, 0, 1, 512)),
              inc=DV)
        em.op("scalar",
              lambda e: e.activation(
                  A("lrow", 512, 0, 1, 512), A("lrow", 1024, 0, 1, 512),
                  AF.Sqrt),
              waits=[(DV, DV.n)], inc=AC)
        em.op("vector",
              lambda e: e.tensor_mul(
                  A("lrow", 1024, 0, 1, 512), A("lrow", 0, 0, 1, 512),
                  A("lrow", 512, 0, 1, 512)),
              waits=[(AC, AC.n)])
        em.op("vector",
              lambda e: e.tensor_scalar(
                  A("lrow", 1024, 0, 1, 512), A("lrow", 1024, 0, 1, 512),
                  -1.0, None, OP.mult),
              inc=DV)
        lnr = DV.n
        em.op("tensor",
              lambda e: e.matmul(
                  bank(5, 0, 512), A("c_or", 0, 0, 1, 128),
                  A("lrow", 512, 0, 1, 512), start=True, stop=True),
              waits=[(DV, lnr)] + bw(5))
        em.op("tensor",
              lambda e: e.matmul(
                  bank(6, 0, 512), A("c_or", 0, 0, 1, 128),
                  A("lrow", 1024, 0, 1, 512), start=True, stop=True),
              waits=bw(6), inc=PE)
        em.op("vector",
              lambda e: e.tensor_copy(A("lbc", 0, 0, 128, 512),
                                      bank(5, 0, 512)),
              waits=[(PE, PE.n)])
        em.op("vector",
              lambda e: e.tensor_copy(A("lbc", 512, 0, 128, 512),
                                      bank(6, 0, 512)),
              inc=DV)
        bset(5, DV)
        bset(6, DV)
        bc_done = DV.n
        for j in range(4):
            for mt in range(CCH):
                em.op("vector",
                      lambda e, mt=mt, j=j: e.tensor_mul(
                          A("fin", 0, 0, 128, 128),
                          A("y", mt * 512 + j * 128, 0, 128, 128),
                          A("lbc", j * 128, 0, 128, 128)),
                      waits=(([(DV, bc_done), (PE, PE.n)]
                              + ([tok_st] if tok_st else []))
                             if mt == 0 else [(PE, PE.n)]))
                em.op("vector",
                      lambda e, mt=mt, j=j: e.tensor_add(
                          A("fin", 0, 0, 128, 128),
                          A("fin", 0, 0, 128, 128),
                          A("lbc", 512 + j * 128, 0, 128, 128)),
                      inc=DV)
                tb = 7 if mt % 2 == 0 else 0
                em.op("tensor",
                      lambda e, tb=tb: e.transpose(
                          bank(tb, 0, 128), A("fin", 0, 0, 128, 128),
                          A("c_id", 0, 0, 128, 128)),
                      waits=[(DV, DV.n)] + bw(tb), inc=PE)
                em.op("scalar",
                      lambda e, mt=mt, tb=tb: e.copy(
                          A("otm", mt * 128, 0, 128, 128), bank(tb, 0, 128)),
                      waits=[(PE, PE.n)], inc=AC)
                bset(tb, AC)
            r0 = ih * 512 + j * 128
            tok_st = dma(d_out[r0:r0 + 128, :], A("otm", 0, 0, 128, C),
                         waits=[(AC, AC.n)], q=qst)

    em.op("gpsimd", lambda e: e.nop(),
          waits=[(s, s.n) for s in qst.sems])

    with nc.Block() as blk:
        blk.sync(lambda e: em.replay("sync", e))
        blk.tensor(lambda e: em.replay("tensor", e))
        blk.vector(lambda e: em.replay("vector", e))
        blk.scalar(lambda e: em.replay("scalar", e))
        blk.gpsimd(lambda e: em.replay("gpsimd", e))

    nc.dbg_sb = sb
    nc.dbg_ps = ps_h
    return nc


def kernel(**inputs):
    x = np.asarray(inputs["L0_attn_x"], dtype=np.float32)
    wq = np.asarray(inputs["L0_attn_Wq"], dtype=np.float32)
    bq = np.asarray(inputs["L0_attn_bq"], dtype=np.float32)
    wk = np.asarray(inputs["L0_attn_Wk"], dtype=np.float32)
    bk = np.asarray(inputs["L0_attn_bk"], dtype=np.float32)
    wv = np.asarray(inputs["L0_attn_Wv"], dtype=np.float32)
    bv = np.asarray(inputs["L0_attn_bv"], dtype=np.float32)
    wo = np.asarray(inputs["L0_attn_Wo"], dtype=np.float32)
    bo = np.asarray(inputs["L0_attn_bo"], dtype=np.float32)
    gate = np.asarray(inputs["L0_ffn_gate_weight"], dtype=np.float32)
    gup = np.asarray(inputs["L0_ffn_gate_up_proj"], dtype=np.float32)
    dwn = np.asarray(inputs["L0_ffn_down_proj"], dtype=np.float32)
    sim = np.asarray(inputs["L0_ffn_similarity"], dtype=np.float32)

    wguT = np.ascontiguousarray(np.transpose(gup, (0, 2, 1)))
    wdT = np.ascontiguousarray(np.transpose(dwn, (0, 2, 1)))
    gateT = np.ascontiguousarray(gate.T)
    simd = sim.copy()
    np.fill_diagonal(simd, 1.0)
    simB = np.ascontiguousarray(
        np.tile(simd.reshape(1, NE * NE), (128, 1)))

    cos_dup, sin_sgn = _rope_tables()
    ident = np.eye(128, dtype=np.float32)
    rot = np.zeros((128, 128), dtype=np.float32)
    for m in range(128):
        rot[(m + 64) % 128, m] = 1.0
    onescol = np.ones((128, 1), dtype=np.float32)
    onesrow = np.ones((1, 128), dtype=np.float32)
    bqc = np.ascontiguousarray(bq.reshape(CCH, 128).T)
    boc = np.ascontiguousarray(bo.reshape(CCH, 128).T)
    bkc = np.ascontiguousarray(bk.reshape(NG, 128).T)
    bvr = np.ascontiguousarray(bv.reshape(1, 512))

    in_maps = []
    for core in range(NCORES):
        b, h = core // 2, core % 2
        _, glob = _glob_tokens(h)
        gl = np.clip(glob, 0, T - 1)
        in_maps.append({
            "xloc": _xloc_for_core(x[b], h),
            "wq": wq, "wk": wk, "wv": wv, "wo": wo,
            "bqc": bqc, "bkc": bkc, "bvr": bvr, "boc": boc,
            "gateT": gateT, "wguT": wguT, "wdT": wdT, "simB": simB,
            "masks": _masks_for_core(h),
            "sinkb": _sinkbias_for_core(h),
            "cosdup": np.ascontiguousarray(cos_dup[:, gl]),
            "sinsgn": np.ascontiguousarray(sin_sgn[:, gl]),
            "ident": ident, "rotmat": rot,
            "onescol": onescol, "onesrow": onesrow,
        })

    nc = build_program()
    res = run_bass_kernel_spmd(nc, in_maps, core_ids=list(range(NCORES)))
    out = np.empty((B, T, C), dtype=np.float32)
    for core in range(NCORES):
        b, h = core // 2, core % 2
        out[b, h * 1024:(h + 1) * 1024] = res.results[core]["out"]
    return out
